# version 07
# baseline (speedup 1.0000x reference)
import numpy as np
import ml_dtypes
import concourse.bass as bass
import concourse.mybir as mybir
from concourse.bass_utils import run_bass_kernel_spmd
from concourse.bass_core_rust import IndirectOffsetOnAxis

F32 = mybir.dt.float32
BF16 = mybir.dt.bfloat16
AF = mybir.ActivationFunctionType
ALU = mybir.AluOpType
AX = mybir.AxisListType
BF = ml_dtypes.bfloat16

NCORES = 8
D = 2048
DC = 16
T = 1024
NH = 16
DH = 128
NSEQ = 8192
NEXP = 16
IMOE = 1408
ISH = 5632
EPS = 1e-5
SOFTCAP = 50.0
SCALE = DH ** -0.5

DBG_FFN = False

QO = 0
KO = 2048 * 1024
VO = 2 * 2048 * 1024
GO = 3 * 2048 * 1024
BLK = GO + 16 * 1024
OBLK = 256 * 8192


def rk(j):
    return (j, 0) if j < 8 else (15 - j, 1)


class Ch:
    def __init__(self, nc, name, step=1):
        self.sem = nc.alloc_semaphore(name)
        self.name = name
        self.n = 0
        self.step = step


class EQ:
    def __init__(self, name):
        self.name = name
        self.items = []

    def wait(self, ch, v=None):
        if v is None:
            v = ch.n
        if v > 0:
            self.items.append(("w", ch, v))

    def op(self, fn):
        self.items.append(("o", fn))

    def inc(self, ch, fn):
        ch.n += ch.step
        self.items.append(("i", fn, ch))
        return ch.n

    def replay(self, eng):
        last = {}
        for it in self.items:
            if it[0] == "w":
                _, ch, v = it
                if last.get(ch.name, 0) < v:
                    eng.wait_ge(ch.sem, v)
                    last[ch.name] = v
            elif it[0] == "o":
                it[1](eng)
            else:
                it[1](eng).then_inc(it[2].sem, it[2].step)


def check_deadlock(queues):
    cnt = {}
    pos = {q.name: 0 for q in queues}
    progress = True
    while progress:
        progress = False
        for q in queues:
            while pos[q.name] < len(q.items):
                it = q.items[pos[q.name]]
                if it[0] == "w":
                    if cnt.get(it[1].name, 0) < it[2]:
                        break
                elif it[0] == "i":
                    ch = it[2]
                    cnt[ch.name] = cnt.get(ch.name, 0) + ch.step
                pos[q.name] += 1
                progress = True
    stuck = [q for q in queues if pos[q.name] < len(q.items)]
    if stuck:
        det = []
        for q in stuck:
            it = q.items[pos[q.name]]
            det.append(f"{q.name}@{pos[q.name]}/{len(q.items)} waits {it[1].name}>={it[2]} (now {cnt.get(it[1].name,0)})")
        raise RuntimeError("DEADLOCK in schedule: " + "; ".join(det))


def build_program(phase="full"):
    P_QKV = phase in ("full", "qkv")
    P_ATTN = phase in ("full", "attn")
    P_FFN = phase in ("full", "ffn")
    nc = bass.Bass("TRN2", target_bir_lowering=False)

    if P_QKV or P_FFN:
        x_t = nc.dram_tensor("x_t", [D, T], F32, kind="ExternalInput")
        rw48 = nc.dram_tensor("rw48", [48, D], F32, kind="ExternalInput")
    if P_QKV:
        w_qkv = nc.dram_tensor("w_qkv", [D, 3 * D], F32, kind="ExternalInput")
        qkv_own = nc.dram_tensor("qkv_own", [BLK // 1024, 1024], BF16,
                                 kind="Internal" if phase == "full" else "ExternalOutput")
    if P_ATTN or P_FFN:
        idxs = nc.dram_tensor("idxs", [128, 4], mybir.dt.int32, kind="ExternalInput")
    if P_ATTN:
        qkvh_all = nc.dram_tensor("qkvh_all", [NCORES * (BLK // 1024), 1024], BF16,
                                  kind="Internal" if phase == "full" else "ExternalInput")
        o_own = nc.dram_tensor("o_own", [OBLK // 1024, 1024], BF16,
                               kind="Internal" if phase == "full" else "ExternalOutput")
    if P_FFN:
        w_out = nc.dram_tensor("w_out", [D, D], F32, kind="ExternalInput")
        w_gate = nc.dram_tensor("w_gate", [NEXP, D, IMOE], F32, kind="ExternalInput")
        w_up = nc.dram_tensor("w_up", [NEXP, D, IMOE], F32, kind="ExternalInput")
        w_down = nc.dram_tensor("w_down", [NEXP, IMOE, D], F32, kind="ExternalInput")
        sh_gate = nc.dram_tensor("sh_gate", [D, ISH], F32, kind="ExternalInput")
        sh_up = nc.dram_tensor("sh_up", [D, ISH], F32, kind="ExternalInput")
        sh_down = nc.dram_tensor("sh_down", [ISH, D], F32, kind="ExternalInput")
        oh_all = nc.dram_tensor("oh_all", [NCORES * (OBLK // 1024), 1024], BF16,
                                kind="Internal" if phase == "full" else "ExternalInput")
        crow = nc.dram_tensor("crow", [17, 1024], BF16)
        y_t = nc.dram_tensor("y_t", [D, T], F32, kind="ExternalOutput")
    DBG = DBG_FFN and phase == "ffn"
    if DBG:
        dbg_x1 = nc.dram_tensor("dbg_x1", [D, T], F32, kind="ExternalOutput")
        dbg_h2 = nc.dram_tensor("dbg_h2", [D, T], BF16, kind="ExternalOutput")
        dbg_glog = nc.dram_tensor("dbg_glog", [17, 1024], F32, kind="ExternalOutput")
        dbg_ltc = nc.dram_tensor("dbg_ltc", [128, 256], F32, kind="ExternalOutput")
        dbg_comt = nc.dram_tensor("dbg_comt", [17, 1024], BF16, kind="ExternalOutput")
        dbg_tpk = nc.dram_tensor("dbg_tpk", [128, 72], F32, kind="ExternalOutput")
        dbg_sst = nc.dram_tensor("dbg_sst", [128, 11264], BF16, kind="ExternalOutput")
        dbg_xr0 = nc.dram_tensor("dbg_xr0", [D, T], F32, kind="ExternalOutput")

    ones_col = nc.inline_tensor(np.ones((128, 1), dtype=BF), "ones_col")
    ones_row = nc.inline_tensor(np.ones((1, 128), dtype=BF), "ones_row")
    id128 = nc.inline_tensor(np.eye(128, dtype=np.float32), "id128")
    id128b = nc.inline_tensor(np.eye(128, dtype=BF), "id128b")
    id16 = nc.inline_tensor(np.eye(16, dtype=np.float32), "id16")
    mks = np.zeros((128, 4 * 512), dtype=BF)
    for jj in range(4):
        kk = np.arange(128)[:, None]
        qq = np.arange(512)[None, :]
        mks[:, 512 * jj:512 * (jj + 1)] = (128 * jj + kk <= qq).astype(BF)
    masks_d = nc.inline_tensor(mks, "masks")

    XR = nc.alloc_sbuf_tensor("XR", [128, DC * T], F32)
    HB = nc.alloc_sbuf_tensor("HB", [128, DC * T], BF16)
    BIGA = nc.alloc_sbuf_tensor("BIGA", [128, 16384], BF16)
    BIGB = nc.alloc_sbuf_tensor("BIGB", [128, 16384], BF16)
    BIGC = nc.alloc_sbuf_tensor("BIGC", [128, 11264], BF16)
    CRL = nc.alloc_sbuf_tensor("CRL", [1, 1024], BF16)
    SH2 = nc.alloc_sbuf_tensor("SH2", [128, 2048], BF16)
    T1 = nc.alloc_sbuf_tensor("T1", [128, 1024], F32)
    BCS = nc.alloc_sbuf_tensor("BCS", [128, 512], BF16)
    LNR = nc.alloc_sbuf_tensor("LNR", [128, 2048], F32)
    COMT = nc.alloc_sbuf_tensor("COMT", [16, 1024], BF16)
    LTC = nc.alloc_sbuf_tensor("LTC", [128, 256], F32)
    NMS = nc.alloc_sbuf_tensor("NMS", [128, 8], F32)
    SMALL = nc.alloc_sbuf_tensor("SMALL", [128, DC * 48], BF16)
    IDXS = nc.alloc_sbuf_tensor("IDXS", [128, 4], mybir.dt.int32)
    ONESC = nc.alloc_sbuf_tensor("ONESC", [128, 1], BF16)
    ONESR = nc.alloc_sbuf_tensor("ONESR", [1, 128], BF16)
    ID128 = nc.alloc_sbuf_tensor("ID128", [128, 128], F32)
    ID128B = nc.alloc_sbuf_tensor("ID128B", [128, 128], BF16)
    ID16 = nc.alloc_sbuf_tensor("ID16", [16, 16], F32)
    TPK = nc.alloc_sbuf_tensor("TPK", [128, 64], F32)

    CS = [BIGC[:, 0:1024], BIGC[:, 1024:2048]]
    STG = [BIGC[:, 2048 + 512 * i: 2048 + 512 * (i + 1)] for i in range(4)]
    SRWO = BIGC[:, 0:512]
    QSL = BIGC[:, 512:1536]
    GRL = BIGC[0:2, 1536:2560]
    SRW = BIGC[0:1, 2560:3072]
    MROWB = CRL
    KTC = HB[:, 0:8192]
    VBC = HB[:, 8192:16384]
    MSK = [BIGC[:, 4096 + 512 * i: 4096 + 512 * (i + 1)] for i in range(4)]
    PTP = [BIGC[:, 6144:7168], BIGC[:, 7168:8192]]
    MBC = SH2[:, 0:1024]
    RBC = SH2[:, 1024:2048]
    CBC = SH2[:, 0:1024]
    R_MEAN = LNR[0:1, 0:1024]
    R_MSQ = LNR[0:1, 1024:2048]
    R_LIV = LNR[0:1, 1024:1536]
    GLOG = LNR[0:16, 0:1024]
    GLOG17 = LNR[0:17, 0:1024]

    PSX0 = nc.alloc_psum_tensor("PSX0", [128, 1024], F32)
    PSX1 = nc.alloc_psum_tensor("PSX1", [128, 1024], F32)
    PS4 = nc.alloc_psum_tensor("PS4", [128, 512], F32)
    PS5 = nc.alloc_psum_tensor("PS5", [128, 512], F32)
    PS6 = nc.alloc_psum_tensor("PS6", [128, 512], F32)
    PS7B = nc.alloc_psum_tensor("PS7B", [128, 1024], BF16)
    PB = [PSX0[:, 0:512], PSX0[:, 512:1024], PSX1[:, 0:512], PSX1[:, 512:1024]]

    cp = Ch(nc, "cp")
    ca = Ch(nc, "ca")
    cv = Ch(nc, "cv")
    cs = Ch(nc, "cs", 16)
    cw = Ch(nc, "cw", 16)
    ccm = Ch(nc, "ccm", 16)

    qt, qa, qv, qs, qg = EQ("qt"), EQ("qa"), EQ("qv"), EQ("qs"), EQ("qg")

    v_x = v_consts = v_idxs = v_smallw = v_sig = v_a2a1 = v_a2a2 = 0
    v_attn_ca_end = v_attn_cv_end = v_attn_cs_end = 0
    v_kv_pe_done = v_linv_done = v_glog = v_comt = 0
    v_ost = {0: 0, 1: 0}
    v_H_all = 0

    qs.inc(cs, lambda e: e.dma_start(out=ONESC[:, :], in_=ones_col[:, :]))
    qs.inc(cs, lambda e: e.dma_start(out=ONESR[:, :], in_=ones_row[:, :]))
    qs.inc(cs, lambda e: e.dma_start(out=ID128[:, :], in_=id128[:, :]))
    qs.inc(cs, lambda e: e.dma_start(out=ID128B[:, :], in_=id128b[:, :]))
    qs.inc(cs, lambda e: e.dma_start(out=ID16[:, :], in_=id16[:, :]))
    v_consts = qs.inc(cs, lambda e: e.dma_start(out=BIGC[:, 4096:6144], in_=masks_d[:, :]))
    qt.wait(cs, v_consts)
    if P_QKV or P_FFN:
        v_x = qs.inc(cs, lambda e: e.dma_start(
            out=XR[:, :].rearrange("p (dc t) -> p dc t", t=T),
            in_=x_t[:, :].rearrange("(dc p) t -> p dc t", p=128)))
    if P_ATTN or P_FFN:
        v_idxs = qg.inc(cw, lambda e: e.dma_start(out=IDXS[:, :], in_=idxs[:, :]))

    RWROWS = BIGB[0:48, 0:2048]
    if P_QKV or P_FFN:
        v_rwrows = qg.inc(cw, lambda e: e.dma_start(out=RWROWS, in_=rw48[:, :]))
        qs.wait(cw, v_rwrows)
        for dc in range(DC):
            v_smallw = qs.inc(cs, lambda e, dc=dc: e.dma_start(
                out=SMALL[:, 48 * dc:48 * (dc + 1)], in_=RWROWS[:, 128 * dc:128 * (dc + 1)],
                transpose=True))

    WQB = [BIGA[:, 0:8192], BIGA[:, 8192:16384]]
    if P_QKV:
        def load_wqkv(b, buf):
            src = w_qkv[:, 512 * b:512 * (b + 1)].rearrange("(dc p) o -> p dc o", p=128)
            dst = buf.rearrange("p (dc o) -> p dc o", o=512)
            return lambda e: e.dma_start(out=dst, in_=src)

        v_wq = {0: qg.inc(cw, load_wqkv(0, WQB[0])), 1: qg.inc(cw, load_wqkv(1, WQB[1]))}

    v_lastCSmm = [0, 0]
    v_pb_evac = {}

    def ln_pass(first):
        v_sq = {}
        vv = 0
        for dc in range(DC):
            sl = dc % 2
            if first and dc < 2:
                qv.wait(cs, v_x)
            qv.wait(cp, v_lastCSmm[sl])
            vc = qv.inc(cv, lambda e, dc=dc, sl=sl: e.tensor_copy(CS[sl][:, :], XR[:, T * dc:T * (dc + 1)]))
            qt.wait(cv, vc)
            for h2 in range(2):
                vv = qt.inc(cp, lambda e, sl=sl, h2=h2, dc=dc: e.matmul(
                    PB[h2][0:1, :], ONESC[:, :], CS[sl][:, 512 * h2:512 * (h2 + 1)],
                    start=(dc == 0), stop=(dc == DC - 1)))
            qa.wait(cp, vv)
            v_sq[dc] = qa.inc(ca, lambda e, sl=sl: e.activation(CS[sl][:, :], CS[sl][:, :], AF.Square))
            qt.wait(ca, v_sq[dc])
            for h2 in range(2):
                v_lastCSmm[sl] = qt.inc(cp, lambda e, sl=sl, h2=h2, dc=dc: e.matmul(
                    PB[2 + h2][0:1, :], ONESC[:, :], CS[sl][:, 512 * h2:512 * (h2 + 1)],
                    start=(dc == 0), stop=(dc == DC - 1)))
        v_sums = v_lastCSmm[1]
        qv.wait(cp, v_sums)
        for h2 in range(2):
            qv.inc(cv, lambda e, h2=h2: e.tensor_scalar_mul(R_MEAN[:, 512 * h2:512 * (h2 + 1)], PB[h2][0:1, :], 1.0 / D))
            v_pb_evac[h2] = cv.n
            qv.inc(cv, lambda e, h2=h2: e.tensor_scalar_mul(R_MSQ[:, 512 * h2:512 * (h2 + 1)], PB[2 + h2][0:1, :], 1.0 / D))
            v_pb_evac[2 + h2] = cv.n
        v_bc_evac = 0
        v_rstd = 0
        for second in (False, True):
            if second:
                qv.wait(cv, v_bc_evac)
                qv.inc(cv, lambda e: e.tensor_mul(R_MEAN[:, :], R_MEAN[:, :], R_MEAN[:, :]))
                qv.inc(cv, lambda e: e.tensor_sub(R_MSQ[:, :], R_MSQ[:, :], R_MEAN[:, :]))
                v_var = qv.inc(cv, lambda e: e.tensor_scalar_add(R_MSQ[:, :], R_MSQ[:, :], EPS))
                qa.wait(cv, v_var)
                qa.inc(ca, lambda e: e.activation(R_MSQ[:, :], R_MSQ[:, :], AF.Ln))
                v_rstd = qa.inc(ca, lambda e: e.activation(R_MSQ[:, :], R_MSQ[:, :], AF.Exp, scale=-0.5))
                qv.wait(ca, v_rstd)
            row = R_MSQ if second else R_MEAN
            dstv = RBC if second else MBC
            v_row = qv.inc(cv, lambda e, row=row: e.tensor_copy(MROWB[:, :], row[:, :]))
            qt.wait(cv, v_row)
            vb = 0
            for h2 in range(2):
                vb = qt.inc(cp, lambda e, h2=h2: e.matmul(
                    PB[h2][:, :], ONESR[:, :], MROWB[:, 512 * h2:512 * (h2 + 1)],
                    start=True, stop=True))
            qv.wait(cp, vb)
            for h2 in range(2):
                v_bc_evac = qv.inc(cv, lambda e, h2=h2, dstv=dstv: e.tensor_copy(
                    dstv[:, 512 * h2:512 * (h2 + 1)], PB[h2][:, :]))
                v_pb_evac[h2] = v_bc_evac
        v_H = 0
        for dc in range(DC):
            qv.wait(cv, v_bc_evac)
            qv.inc(cv, lambda e, dc=dc: e.tensor_sub(HB[:, T * dc:T * (dc + 1)], XR[:, T * dc:T * (dc + 1)], MBC[:, :]))
            v_H = qv.inc(cv, lambda e, dc=dc: e.tensor_mul(HB[:, T * dc:T * (dc + 1)], HB[:, T * dc:T * (dc + 1)], RBC[:, :]))
        return v_H

    if P_QKV:
        v_H_all = ln_pass(first=True)

    stg_i = [0]
    v_stg_dma = {}
    mmrot = [0]
    v_blk_use = {}

    if P_QKV:

        def own_qk_ap(off_rows, oc, h2):
            return qkv_own[off_rows + 128 * oc: off_rows + 128 * (oc + 1), 512 * h2:512 * (h2 + 1)]

        def stg_store(vmm, pb, bank, dsts):
            sl = stg_i[0] % 4
            stg_i[0] += 1
            qv.wait(cp, vmm)
            if sl in v_stg_dma:
                qv.wait(cs, v_stg_dma[sl])
            vcp = qv.inc(cv, lambda e, sl=sl, pb=pb: e.tensor_copy(STG[sl][:, :], pb[:, :]))
            v_pb_evac[bank] = vcp
            qs.wait(cv, vcp)
            for (dst, c0, c1) in dsts:
                v_stg_dma[sl] = qs.inc(cs, lambda e, sl=sl, dst=dst, c0=c0, c1=c1: e.dma_start(
                    out=dst, in_=STG[sl][:, c0:c1]))

        for b in range(12):
            buf = b % 2
            qt.wait(cw, v_wq[b])
            qt.wait(cv, v_H_all)
            for ot in range(4):
                oc = 4 * (b % 4) + ot
                for h2 in range(2):
                    bank = mmrot[0] % 4
                    mmrot[0] += 1
                    pb = PB[bank]
                    qt.wait(cv, v_pb_evac.get(bank, 0))
                    vmm = 0
                    for dc in range(DC):
                        vmm = qt.inc(cp, lambda e, buf=buf, dc=dc, ot=ot, h2=h2, pb=pb: e.matmul(
                            pb[:, :], WQB[buf][:, 512 * dc + 128 * ot: 512 * dc + 128 * (ot + 1)],
                            HB[:, T * dc + 512 * h2: T * dc + 512 * h2 + 512],
                            start=(dc == 0), stop=(dc == DC - 1)))
                    v_blk_use[b] = vmm
                    off_rows = 2048 * (b // 4)
                    stg_store(vmm, pb, bank, [(own_qk_ap(off_rows, oc, h2), 0, 512)])
            if b + 2 < 12:
                qg.wait(cp, v_blk_use[b])
                v_wq[b + 2] = qg.inc(cw, load_wqkv(b + 2, WQB[buf]))

        qt.wait(cs, v_smallw)
        qt.wait(cv, v_pb_evac.get(0, 0))
        v_gmm = 0
        for h2 in range(2):
            for dc in range(DC):
                v_gmm = qt.inc(cp, lambda e, dc=dc, h2=h2: e.matmul(
                    (PS6 if h2 == 0 else PB[0])[0:16, :], SMALL[:, 48 * dc + 17:48 * dc + 33],
                    HB[:, T * dc + 512 * h2: T * dc + 512 * h2 + 512],
                    start=(dc == 0), stop=(dc == DC - 1)))
        qa.wait(cp, v_gmm)
        v_sig = 0
        for h2 in range(2):
            v_sig = qa.inc(ca, lambda e, h2=h2: e.activation(
                COMT[:, 512 * h2:512 * (h2 + 1)], (PS6 if h2 == 0 else PB[0])[0:16, :], AF.Sigmoid))
        qs.wait(ca, v_sig)
        qs.inc(cs, lambda e: e.dma_start(out=qkv_own[6144:6160, :], in_=COMT[:, :]))
        if phase == "qkv":
            qs.wait(cs, cs.n)

    if phase == "full":
        qg.wait(cs, cs.n)
        v_a2a1 = qg.inc(ccm, lambda e: e.collective_compute(
            "AllGather", ALU.bypass, replica_groups=rgroups,
            ins=[qkv_own.ap().opt()], outs=[qkvh_all.ap().opt()]))

    if P_FFN:
        WOB = [BIGB[:, 0:8192], BIGB[:, 8192:16384]]

        def load_wout(b, buf):
            src = w_out[:, 512 * b:512 * (b + 1)].rearrange("(hc p) o -> p hc o", p=128)
            dst = buf.rearrange("p (hc o) -> p hc o", o=512)
            return lambda e: e.dma_start(out=dst, in_=src)

        qg.wait(cs, v_smallw)
        v_wo = {0: qg.inc(cw, load_wout(0, WOB[0])), 1: qg.inc(cw, load_wout(1, WOB[1]))}

    if P_ATTN:
        RPB = BLK // 1024
        QKV512 = qkvh_all[:, :].rearrange("r (a c) -> (r a) c", c=512)

        def gather_qkvg(e, out_ap, idx_ap, row_const, sub):
            return e.indirect_dma_start(
                out_ap, None, QKV512, IndirectOffsetOnAxis(idx_ap, 0),
                element_offset=(2 * row_const + sub) * 512)

        v_kv = {}
        v_kld = {}
        v_qld = {}
        ZST = [BIGC[:, 3072:3584], BIGC[:, 3584:4096]]
        v_ztr_pe = {}
        v_ztr_cv = 0
        v_gld = {}
        v_q_use = {0: 0, 1: 0}
        v_g_use = {0: 0, 1: 0}
        v_kv_pe_done = 0
        psO = [PS4, PS5]
        psL = PS6
        v_ost = {0: 0, 1: 0}
        v_linv_done = 0
        v_ppair_last = {0: 0, 1: 0}
        v_tanh_last = {0: 0, 1: 0}
        v_psx_war_cv = {0: 0, 1: 0}
        v_srwo_cs = 0
        pfirst = True
        vpv = 0

        OV = o_own[:, :].rearrange("(rl e8) c -> rl (e8 c)", e8=8)

        qt.wait(ca, v_sig)

        for hl in range(2):
            for j in range(16):
                nkt = 4 * j + 4
                jsl = j % 2
                r, sub = rk(j)
                if pfirst:
                    qg.wait(ccm, v_a2a1)
                    qg.wait(cw, v_idxs)
                    pfirst = False
                if j == 0:
                    qg.wait(cp, v_kv_pe_done)
                qg.wait(cp, v_q_use[jsl])
                v_qld[(hl, j)] = qg.inc(cw, lambda e, jsl=jsl, r=r, sub=sub, hl=hl: gather_qkvg(
                    e, QSL[:, 512 * jsl:512 * jsl + 512], IDXS[:, 0:1],
                    RPB * r + 128 * hl, sub))
                qg.wait(cv, v_g_use[jsl])
                v_gld[(hl, j)] = qg.inc(cw, lambda e, jsl=jsl, r=r, sub=sub, hl=hl: gather_qkvg(
                    e, GRL[0:2, 512 * jsl:512 * jsl + 512], IDXS[0:2, 3:4],
                    RPB * r + 6144 + hl, sub))
                v_kld[(hl, j)] = qg.inc(cw, lambda e, j=j, r=r, sub=sub, hl=hl: gather_qkvg(
                    e, KTC[:, 512 * j:512 * (j + 1)], IDXS[:, 0:1],
                    RPB * r + 2048 + 128 * hl, sub))
                zsl = (2 * hl + j) % 2
                qg.wait(cp, v_ztr_pe.get(zsl, 0))
                v_vld = qg.inc(cw, lambda e, zsl=zsl, r=r, sub=sub, hl=hl: gather_qkvg(
                    e, ZST[zsl][:, :], IDXS[:, 0:1],
                    RPB * r + 4096 + 128 * hl, sub))
                qt.wait(cw, v_vld)
                qt.wait(cv, v_ztr_cv)
                for kt4 in range(4):
                    v_ztr_pe[zsl] = qt.inc(cp, lambda e, kt4=kt4, zsl=zsl: e.transpose(
                        PS7B[:, 128 * kt4: 128 * (kt4 + 1)],
                        ZST[zsl][:, 128 * kt4:128 * (kt4 + 1)], ID128B[:, :]))
                qv.wait(cp, v_ztr_pe[zsl])
                v_ztr_cv = qv.inc(cv, lambda e, j=j: e.tensor_copy(
                    VBC[:, 512 * j:512 * (j + 1)], PS7B[:, 0:512]))
                v_kv[(hl, j)] = v_ztr_cv

                par = j % 2
                for kp in range(nkt // 2):
                    kts = (2 * kp, 2 * kp + 1)
                    rot = kp % 2
                    PSX = PSX0 if rot == 0 else PSX1
                    qt.wait(cw, v_kld[(hl, kts[1] // 4)])
                    qt.wait(cv, v_kv[(hl, kts[1] // 4)])
                    qt.wait(cw, v_qld[(hl, j)])
                    qt.wait(ca, v_tanh_last[rot])
                    qt.wait(cv, max(v_psx_war_cv[rot], v_pb_evac.get(2 * rot, 0), v_pb_evac.get(2 * rot + 1, 0)))
                    vs = 0
                    for ii, kt in enumerate(kts):
                        vs = qt.inc(cp, lambda e, kt=kt, ii=ii, PSX=PSX, jsl=jsl: e.matmul(
                            PSX[:, 512 * ii:512 * (ii + 1)],
                            KTC[:, 128 * kt:128 * (kt + 1)],
                            QSL[:, 512 * jsl:512 * jsl + 512], start=True, stop=True))
                    qa.wait(cp, vs)
                    qa.wait(cp, v_ppair_last[rot])
                    qa.inc(ca, lambda e, PSX=PSX: e.activation(T1[:, :], PSX[:, :], AF.Tanh, scale=SCALE / SOFTCAP))
                    v_tanh_last[rot] = ca.n
                    v_pp = qa.inc(ca, lambda e, rot=rot: e.activation(PTP[rot][:, :], T1[:, :], AF.Exp, scale=SOFTCAP))
                    vmsk = 0
                    for ii, kt in enumerate(kts):
                        if kt >= 4 * j:
                            qv.wait(ca, v_pp)
                            qv.wait(cs, v_consts)
                            vmsk = qv.inc(cv, lambda e, rot=rot, ii=ii, kt=kt, j=j: e.tensor_mul(
                                PTP[rot][:, 512 * ii:512 * (ii + 1)],
                                PTP[rot][:, 512 * ii:512 * (ii + 1)], MSK[kt - 4 * j][:, :]))
                    qt.wait(ca, v_pp)
                    if vmsk:
                        qt.wait(cv, vmsk)
                    if kp == 0:
                        qt.wait(cv, v_linv_done)
                        qt.wait(cv, v_ost[par])
                    for ii, kt in enumerate(kts):
                        qt.inc(cp, lambda e, rot=rot, ii=ii, kt=kt, nkt=nkt: e.matmul(
                            psL[0:1, :], ONESC[:, :], PTP[rot][:, 512 * ii:512 * (ii + 1)],
                            start=(kt == 0), stop=(kt == nkt - 1)))
                        vpv = qt.inc(cp, lambda e, rot=rot, ii=ii, kt=kt, nkt=nkt, par=par: e.matmul(
                            psO[par][:, :], VBC[:, 128 * kt:128 * (kt + 1)],
                            PTP[rot][:, 512 * ii:512 * (ii + 1)],
                            start=(kt == 0), stop=(kt == nkt - 1)))
                    v_ppair_last[rot] = vpv
                v_q_use[jsl] = vpv
                v_kv_pe_done = vpv
                qv.wait(cp, vpv)
                v_linv_done = qv.inc(cv, lambda e: e.reciprocal(R_LIV[:, :], psL[0:1, :]))
                qv.wait(cw, v_gld[(hl, j)])
                qv.wait(cv, v_linv_done)
                v_srow = qv.inc(cv, lambda e, jsl=jsl: e.tensor_mul(
                    SRW[:, :], GRL[0:1, 512 * jsl:512 * jsl + 512], R_LIV[:, :]))
                v_g_use[jsl] = v_srow
                qt.wait(cv, v_srow)
                brot = (((nkt // 2) - 1) % 2) ^ 1
                PSXb = PSX0 if brot == 0 else PSX1
                qt.wait(ca, v_tanh_last[brot])
                v_bc = qt.inc(cp, lambda e, PSXb=PSXb: e.matmul(
                    PSXb[:, 0:512], ONESR[:, :], SRW[:, :], start=True, stop=True))
                qv.wait(cp, v_bc)
                v_bcs = qv.inc(cv, lambda e, PSXb=PSXb: e.tensor_copy(BCS[:, :], PSXb[:, 0:512]))
                v_psx_war_cv[brot] = v_bcs
                qv.wait(cs, v_srwo_cs)
                v_ost[par] = qv.inc(cv, lambda e, par=par: e.scalar_tensor_tensor(
                    SRWO[:, :], psO[par][:, :], 1.0, BCS[:, :], ALU.mult, ALU.mult))
                qs.wait(cv, v_ost[par])
                v_srwo_cs = qs.inc(cs, lambda e, j=j, hl=hl: e.dma_start(
                    out=OV[128 * hl:128 * (hl + 1), 512 * j:512 * (j + 1)],
                    in_=SRWO[:, :]))
        v_attn_ca_end = ca.n
        v_attn_cv_end = cv.n
        v_attn_cs_end = cs.n
        if phase == "attn":
            qs.wait(cs, cs.n)

    if phase == "full":
        qg.wait(cs, cs.n)
        v_a2a2 = qg.inc(ccm, lambda e: e.collective_compute(
            "AllGather", ALU.bypass, replica_groups=rgroups,
            ins=[o_own.ap().opt()], outs=[oh_all.ap().opt()]))

    if P_FFN:
        OH512 = oh_all[:, :].rearrange("r (a c) -> (r a) c", c=512)
        qv.wait(cs, v_x)
        qg.wait(ccm, v_a2a2)
        qg.wait(cw, v_idxs)
        v_oload = 0
        for hc in range(DC):
            for (ti, icol) in ((0, 1), (1, 2)):
                v_oload = qg.inc(cw, lambda e, hc=hc, ti=ti, icol=icol: e.indirect_dma_start(
                    BIGA[:, T * hc + 512 * ti: T * hc + 512 * ti + 512], None,
                    OH512, IndirectOffsetOnAxis(IDXS[:, icol:icol + 1], 0),
                    element_offset=hc * 2048 * 512))

        v_wo_use = {}
        v_res_last = 0
        qt.wait(ca, v_attn_ca_end)
        qt.wait(cv, v_attn_cv_end)
        for b in range(4):
            buf = b % 2
            qt.wait(cw, v_wo[b])
            qt.wait(cw, v_oload)
            for ot in range(4):
                oc = 4 * b + ot
                for h2 in range(2):
                    bank = mmrot[0] % 4
                    mmrot[0] += 1
                    pb = PB[bank]
                    qt.wait(cv, v_pb_evac.get(bank, 0))
                    vmm = 0
                    for hc in range(DC):
                        vmm = qt.inc(cp, lambda e, buf=buf, hc=hc, ot=ot, h2=h2, pb=pb: e.matmul(
                            pb[:, :], WOB[buf][:, 512 * hc + 128 * ot: 512 * hc + 128 * (ot + 1)],
                            BIGA[:, T * hc + 512 * h2: T * hc + 512 * h2 + 512],
                            start=(hc == 0), stop=(hc == DC - 1)))
                    v_wo_use[b] = vmm
                    qv.wait(cp, vmm)
                    v_res_last = qv.inc(cv, lambda e, oc=oc, h2=h2, pb=pb: e.tensor_tensor(
                        XR[:, T * oc + 512 * h2: T * oc + 512 * h2 + 512],
                        XR[:, T * oc + 512 * h2: T * oc + 512 * h2 + 512], pb[:, :], ALU.add))
                    v_pb_evac[bank] = v_res_last
            if b + 2 < 4:
                qg.wait(cp, v_wo_use[b])
                v_wo[b + 2] = qg.inc(cw, load_wout(b + 2, WOB[buf]))
        v_outproj_pe = v_wo_use[3]

        if DBG:
            qs.wait(cv, v_res_last)
            v_dbg_x1 = qs.inc(cs, lambda e: e.dma_start(
                out=dbg_x1[:, :].rearrange("(dc p) t -> p dc t", p=128),
                in_=XR[:, :].rearrange("p (dc t) -> p dc t", t=T)))
            qv.wait(cs, v_dbg_x1)

        qv.wait(cs, v_attn_cs_end)
        v_H2_all = ln_pass(first=False)
        v_ln2_stats = v_lastCSmm[1]

        if DBG:
            qs.wait(cv, v_H2_all)
            qs.inc(cs, lambda e: e.dma_start(
                out=dbg_h2[:, :].rearrange("(dc p) t -> p dc t", p=128),
                in_=HB[:, :].rearrange("p (dc t) -> p dc t", t=T)))

        v_rmm = 0
        qt.wait(cv, v_H2_all)
        qt.wait(cv, v_linv_done)
        for h2 in range(2):
            for dc in range(DC):
                v_rmm = qt.inc(cp, lambda e, dc=dc, h2=h2: e.matmul(
                    (PS6 if h2 == 0 else PB[0])[0:17, :], SMALL[:, 48 * dc:48 * dc + 17],
                    HB[:, T * dc + 512 * h2: T * dc + 512 * h2 + 512],
                    start=(dc == 0), stop=(dc == DC - 1)))
        qv.wait(cp, v_rmm)
        v_glog = 0
        for h2 in range(2):
            v_glog = qv.inc(cv, lambda e, h2=h2: e.tensor_copy(
                GLOG17[:, 512 * h2:512 * (h2 + 1)], (PS6 if h2 == 0 else PB[0])[0:17, :]))
        if DBG:
            qs.wait(cv, v_glog)
            qs.inc(cs, lambda e: e.dma_start(out=dbg_glog[:, :], in_=GLOG17[:, :]))
        SGS = BIGB[0:17, 0:1024]
        qa.wait(cp, v_rmm)
        v_sg2 = 0
        for h2 in range(2):
            v_sg2 = qa.inc(ca, lambda e, h2=h2: e.activation(
                SGS[:, 512 * h2:512 * (h2 + 1)], (PS6 if h2 == 0 else PB[0])[0:17, :], AF.Sigmoid))
        qs.wait(ca, v_sg2)
        v_sg_dma = qs.inc(cs, lambda e: e.dma_start(out=crow[16:17, :], in_=BIGB[16:17, 0:1024]))

        LTALL = LTC[:, 0:128]
        CMBA = LTC[:, 128:256]
        PRB = TPK[:, 0:16]
        WKT = TPK[:, 16:32]
        MKT = TPK[:, 32:48]
        KPT = TPK[:, 48:64]
        NM = NMS[:, 0:1]
        SM = NMS[:, 1:2]
        RS1 = NMS[:, 2:3]
        MXV = NMS[:, 3:4]
        T4 = NMS[:, 4:5]
        KS = NMS[:, 5:6]
        RD = NMS[:, 6:7]
        qt.wait(cv, v_glog)
        v_lt = 0
        for jtl in range(8):
            v_lt = qt.inc(cp, lambda e, jtl=jtl: e.transpose(
                PB[2][:, 16 * jtl:16 * (jtl + 1)], GLOG[:, 128 * jtl:128 * (jtl + 1)], ID16[:, :]))
        qv.wait(cp, v_lt)
        v_ltall = qv.inc(cv, lambda e: e.tensor_copy(LTALL[:, :], PB[2][:, 0:128]))
        def qvs(fn):
            v = qv.inc(cv, fn)
            qv.wait(cv, v)
            return v

        for jtl in range(8):
            LT = LTALL[:, 16 * jtl:16 * (jtl + 1)]
            CMB = CMBA[:, 16 * jtl:16 * (jtl + 1)]
            v = qvs(lambda e, LT=LT: e.reduce_max(NM[:, :], LT[:, :], axis=AX.X, negate=True))
            qa.wait(cv, v)
            v = qa.inc(ca, lambda e, LT=LT: e.activation(PRB[:, :], LT[:, :], AF.Exp, bias=NM[:, :]))
            qv.wait(ca, v)
            qvs(lambda e: e.reduce_sum(SM[:, :], PRB[:, :], axis=AX.X))
            qvs(lambda e: e.reciprocal(RS1[:, :], SM[:, :]))
            qvs(lambda e: e.tensor_scalar_mul(PRB[:, :], PRB[:, :], RS1[:, :]))
            qvs(lambda e: e.tensor_copy(WKT[:, :], PRB[:, :]))
            for _ in range(3):
                qvs(lambda e: e.reduce_max(MXV[:, :], WKT[:, :], axis=AX.X))
                qvs(lambda e: e.tensor_scalar(MKT[:, :], WKT[:, :], MXV[:, :], None, ALU.is_ge))
                qvs(lambda e: e.scalar_tensor_tensor(WKT[:, :], MKT[:, :], -2.0, WKT[:, :], ALU.mult, ALU.add))
            qvs(lambda e: e.reduce_max(T4[:, :], WKT[:, :], axis=AX.X))
            qvs(lambda e: e.tensor_scalar(MKT[:, :], PRB[:, :], T4[:, :], None, ALU.is_ge))
            qvs(lambda e: e.tensor_mul(KPT[:, :], PRB[:, :], MKT[:, :]))
            qvs(lambda e: e.reduce_sum(KS[:, :], KPT[:, :], axis=AX.X))
            qvs(lambda e: e.tensor_scalar_max(KS[:, :], KS[:, :], 1e-9))
            qvs(lambda e: e.reciprocal(RD[:, :], KS[:, :]))
            v_cmb = qvs(lambda e, CMB=CMB: e.tensor_scalar_mul(CMB[:, :], KPT[:, :], RD[:, :]))
        if DBG:
            qs.wait(cv, v_cmb)
            qs.inc(cs, lambda e: e.dma_start(out=dbg_ltc[:, :], in_=LTC[:, :]))
            qs.inc(cs, lambda e: e.dma_start(out=dbg_tpk[:, 0:64], in_=TPK[:, :]))
            qs.inc(cs, lambda e: e.dma_start(out=dbg_tpk[:, 64:72], in_=NMS[:, :]))
        v_ct_last = 0
        qt.wait(cv, v_cmb)
        for jtl in range(8):
            pbank = PB[3] if jtl < 4 else PB[2]
            v_ct_last = qt.inc(cp, lambda e, jtl=jtl, pbank=pbank: e.transpose(
                pbank[0:16, 128 * (jtl % 4):128 * (jtl % 4 + 1)],
                CMBA[:, 16 * jtl:16 * (jtl + 1)], ID128[:, :]))
        qv.wait(cp, v_ct_last)
        qv.wait(cs, v_sg_dma)
        qv.inc(cv, lambda e: e.tensor_copy(COMT[:, 0:512], PB[3][0:16, :]))
        v_comt = qv.inc(cv, lambda e: e.tensor_copy(COMT[:, 512:1024], PB[2][0:16, :]))
        qs.wait(cv, v_comt)
        v_crow_dma = qs.inc(cs, lambda e: e.dma_start(out=crow[0:16, :], in_=COMT[:, :]))
        if DBG:
            qs.inc(cs, lambda e: e.dma_start(out=dbg_comt[0:16, :], in_=COMT[:, :]))
            qs.wait(ca, v_sg2)
            v_dbg_sgs = qs.inc(cs, lambda e: e.dma_start(out=dbg_comt[16:17, :], in_=BIGB[16:17, 0:1024]))
            qa.wait(cs, v_dbg_sgs)

        WGU = [BIGA[:, 0:8192], BIGA[:, 8192:16384]]
        WDB = [BIGC[:, 0:5632], BIGC[:, 5632:11264]]
        IBLK = [(0, 512), (512, 512), (1024, 384)]
        NSUB = [4, 4, 3]

        def unit_gate_src(u, ib):
            i0, bw = IBLK[ib]
            if u < NEXP:
                return w_gate[u, :, i0:i0 + bw].rearrange("(dc p) i -> p dc i", p=128)
            sb = u - NEXP
            return sh_gate[:, 1408 * sb + i0: 1408 * sb + i0 + bw].rearrange("(dc p) i -> p dc i", p=128)

        def unit_up_src(u, ib):
            i0, bw = IBLK[ib]
            if u < NEXP:
                return w_up[u, :, i0:i0 + bw].rearrange("(dc p) i -> p dc i", p=128)
            sb = u - NEXP
            return sh_up[:, 1408 * sb + i0: 1408 * sb + i0 + bw].rearrange("(dc p) i -> p dc i", p=128)

        def unit_down_src(u, b):
            if u < NEXP:
                return w_down[u, :, 512 * b:512 * (b + 1)].rearrange("(ic p) o -> p ic o", p=128)
            sb = u - NEXP
            return sh_down[1408 * sb:1408 * (sb + 1), 512 * b:512 * (b + 1)].rearrange("(ic p) o -> p ic o", p=128)

        def pass_src(u, p):
            return unit_gate_src(u, p // 2) if p % 2 == 0 else unit_up_src(u, p // 2)

        SST = BIGB

        def gu_dst(buf, bw):
            return WGU[buf][:, 0:DC * bw].rearrange("p (dc i) -> p dc i", i=bw)

        def wd_dst(buf):
            return WDB[buf].rearrange("p (ic o) -> p ic o", o=512)

        v_gu_load = {}
        v_gu_use = {}
        v_d_load = {}
        v_d_use = {}
        v_silu = {}
        v_umul = {}
        v_sc = {}
        v_gu_evac = {0: ("v", v_comt), 1: ("v", v_comt), 2: ("v", v_comt), 3: ("v", v_comt)}
        DPS = (PS4, PS5, PS6)
        v_d_evac = {0: v_ost[0], 1: v_ost[1], 2: v_comt}
        qt.wait(ca, v_sg2)
        qt.wait(cv, v_comt)
        v_cbc = 0
        v_unit_pe_done = v_outproj_pe
        gpass = [(u, p) for u in range(20) for p in range(6)]

        qa.wait(cs, v_sg_dma)
        qg.wait(cp, v_outproj_pe)
        v_gu_load[(0, 0)] = qg.inc(cw, lambda e: e.dma_start(out=gu_dst(0, 512), in_=unit_gate_src(0, 0)))
        v_gu_load[(0, 1)] = qg.inc(cw, lambda e: e.dma_start(out=gu_dst(1, 512), in_=unit_up_src(0, 0)))

        for u in range(20):
            if u <= NEXP:
                rowidx = u if u < NEXP else 16
                qs.wait(cs, max(v_crow_dma, v_sg_dma))
                qs.wait(cv, v_cbc)
                v_crl = qs.inc(cs, lambda e, rowidx=rowidx: e.dma_start(out=CRL[:, :], in_=crow[rowidx:rowidx + 1, :]))
                qt.wait(cs, v_crl)
                for bank in (0, 1):
                    kind, vev = v_gu_evac[bank]
                    qt.wait(ca if kind == "a" else cv, vev)
                vb = 0
                for h2 in range(2):
                    vb = qt.inc(cp, lambda e, h2=h2: e.matmul(
                        PB[h2][:, :], ONESR[:, :], CRL[:, 512 * h2:512 * (h2 + 1)], start=True, stop=True))
                qv.wait(cp, vb)
                for h2 in range(2):
                    v_cbc = qv.inc(cv, lambda e, h2=h2: e.tensor_copy(CBC[:, 512 * h2:512 * (h2 + 1)], PB[h2][:, :]))
                v_gu_evac[0] = ("v", v_cbc)
                v_gu_evac[1] = ("v", v_cbc)
            for p in range(6):
                ib = p // 2
                i0, bw = IBLK[ib]
                isG = (p % 2 == 0)
                buf = p % 2
                qt.wait(cw, v_gu_load[(u, p)])
                for isub in range(NSUB[ib]):
                    ic = 4 * ib + isub
                    for h2 in range(2):
                        bank = (0 if isG else 2) + h2
                        kind, vev = v_gu_evac[bank]
                        qt.wait(ca if kind == "a" else cv, vev)
                        vmm = 0
                        for dc in range(DC):
                            vmm = qt.inc(cp, lambda e, buf=buf, dc=dc, bw=bw, isub=isub, h2=h2, bank=bank: e.matmul(
                                PB[bank][:, :],
                                WGU[buf][:, bw * dc + 128 * isub: bw * dc + 128 * isub + 128],
                                HB[:, T * dc + 512 * h2: T * dc + 512 * h2 + 512],
                                start=(dc == 0), stop=(dc == DC - 1)))
                        v_gu_use[(u, p)] = vmm
                        if isG:
                            qa.wait(cp, vmm)
                            qa.wait(cp, v_unit_pe_done)
                            v_silu[(ic, h2)] = qa.inc(ca, lambda e, ic=ic, h2=h2, bank=bank: e.activation(
                                SST[:, T * ic + 512 * h2: T * ic + 512 * h2 + 512], PB[bank][:, :], AF.Silu))
                            v_gu_evac[bank] = ("a", v_silu[(ic, h2)])
                        else:
                            qv.wait(cp, vmm)
                            qv.wait(ca, v_silu[(ic, h2)])
                            v_umul[(ic, h2)] = qv.inc(cv, lambda e, ic=ic, h2=h2, bank=bank: e.tensor_tensor(
                                SST[:, T * ic + 512 * h2: T * ic + 512 * h2 + 512],
                                SST[:, T * ic + 512 * h2: T * ic + 512 * h2 + 512],
                                PB[bank][:, :], ALU.mult))
                            v_gu_evac[bank] = ("v", v_umul[(ic, h2)])
                gi = 6 * u + p
                if gi + 2 < len(gpass):
                    nu, np_ = gpass[gi + 2]
                    qg.wait(cp, v_gu_use[(u, p)])
                    v_gu_load[(nu, np_)] = qg.inc(cw, lambda e, nu=nu, np_=np_, buf=buf: e.dma_start(
                        out=gu_dst(buf, IBLK[np_ // 2][1]), in_=pass_src(nu, np_)))
                if p == 1:
                    for bb in range(2):
                        dbuf = bb % 2
                        if u == 0:
                            qg.wait(cp, v_ln2_stats)
                            qg.wait(cp, v_kv_pe_done)
                            qg.wait(ca, v_attn_ca_end)
                            qg.wait(cv, v_attn_cv_end)
                            qg.wait(cs, v_consts)
                        if u > 0:
                            qg.wait(cp, v_d_use.get((u - 1, 2 + bb), 0))
                        v_d_load[(u, bb)] = qg.inc(cw, lambda e, u=u, bb=bb, dbuf=dbuf: e.dma_start(
                            out=wd_dst(dbuf), in_=unit_down_src(u, bb)))
            for ic in range(11):
                qv.wait(cv, v_umul[(ic, 1)])
                qv.wait(cv, v_cbc)
                v_sc[ic] = qv.inc(cv, lambda e, ic=ic: e.tensor_mul(
                    SST[:, T * ic:T * (ic + 1)], SST[:, T * ic:T * (ic + 1)], CBC[:, :]))
            if DBG and u == 0:
                qs.wait(cv, v_sc[10])
                v_dbg_sst = qs.inc(cs, lambda e: e.dma_start(out=dbg_sst[:, :], in_=SST[:, 0:11264]))
                qa.wait(cs, v_dbg_sst)
            for b in range(4):
                dbuf = b % 2
                qt.wait(cw, v_d_load[(u, b)])
                qt.wait(cv, v_sc[10])
                for dcl in range(4):
                    dc = 4 * b + dcl
                    for h2 in range(2):
                        dbank = (2 * dcl + h2) % 3
                        qt.wait(cv, v_d_evac[dbank])
                        vmm = 0
                        for ic in range(11):
                            vmm = qt.inc(cp, lambda e, dbuf=dbuf, ic=ic, dcl=dcl, h2=h2, dbank=dbank: e.matmul(
                                DPS[dbank][:, :],
                                WDB[dbuf][:, 512 * ic + 128 * dcl: 512 * ic + 128 * dcl + 128],
                                SST[:, T * ic + 512 * h2: T * ic + 512 * h2 + 512],
                                start=(ic == 0), stop=(ic == 10)))
                        v_d_use[(u, b)] = vmm
                        qv.wait(cp, vmm)
                        v_d_evac[dbank] = qv.inc(cv, lambda e, dc=dc, h2=h2, dbank=dbank: e.tensor_tensor(
                            XR[:, T * dc + 512 * h2: T * dc + 512 * h2 + 512],
                            XR[:, T * dc + 512 * h2: T * dc + 512 * h2 + 512],
                            DPS[dbank][:, :], ALU.add))
                if b + 2 < 4:
                    qg.wait(cp, v_d_use[(u, b)])
                    v_d_load[(u, b + 2)] = qg.inc(cw, lambda e, u=u, b=b, dbuf=dbuf: e.dma_start(
                        out=wd_dst(dbuf), in_=unit_down_src(u, b + 2)))
            v_unit_pe_done = v_d_use[(u, 3)]
            if DBG and u == 0:
                qs.wait(cv, max(v_d_evac.values()))
                v_dbg_xr0 = qs.inc(cs, lambda e: e.dma_start(
                    out=dbg_xr0[:, :].rearrange("(dc p) t -> p dc t", p=128),
                    in_=XR[:, :].rearrange("p (dc t) -> p dc t", t=T)))
                qv.wait(cs, v_dbg_xr0)

        v_final = max(v_d_evac.values())

        qs.wait(cv, v_final)
        v_y = qs.inc(cs, lambda e: e.dma_start(
            out=y_t[:, :].rearrange("(dc p) t -> p dc t", p=128),
            in_=XR[:, :].rearrange("p (dc t) -> p dc t", t=T)))
        qs.wait(cs, v_y)

    check_deadlock([qt, qa, qv, qs, qg])

    with nc.Block() as block:
        @block.tensor
        def _(e):
            qt.replay(e)

        @block.scalar
        def _(e):
            qa.replay(e)

        @block.vector
        def _(e):
            qv.replay(e)

        @block.sync
        def _(e):
            qs.replay(e)

        @block.gpsimd
        def _(e):
            qg.replay(e)

    return nc


_NC_CACHE = None


def kernel(**inputs):
    global _NC_CACHE
    x = np.asarray(inputs["L0_attn_x"], dtype=np.float32)
    w_qkv = np.asarray(inputs["L0_attn_w_qkv"], dtype=np.float32)
    w_gates = np.asarray(inputs["L0_attn_w_gates"], dtype=np.float32)
    w_out = np.asarray(inputs["L0_attn_w_out"], dtype=np.float32)
    router_w = np.asarray(inputs["L0_ffn_router_w"], dtype=np.float32)
    w_gate = np.asarray(inputs["L0_ffn_w_gate"], dtype=np.float32)
    w_up = np.asarray(inputs["L0_ffn_w_up"], dtype=np.float32)
    w_down = np.asarray(inputs["L0_ffn_w_down"], dtype=np.float32)
    sh_gate = np.asarray(inputs["L0_ffn_sh_gate"], dtype=np.float32)
    sh_up = np.asarray(inputs["L0_ffn_sh_up"], dtype=np.float32)
    sh_down = np.asarray(inputs["L0_ffn_sh_down"], dtype=np.float32)
    sh_gate_w = np.asarray(inputs["L0_ffn_sh_gate_w"], dtype=np.float32)
    rw48 = np.zeros((48, D), dtype=np.float32)
    rw48[0:16] = router_w
    rw48[16] = sh_gate_w[:, 0]
    rw48[17:33] = w_gates.T
    pp = np.arange(128, dtype=np.int32)

    if _NC_CACHE is None:
        _NC_CACHE = {ph: build_program(ph) for ph in ("qkv", "attn", "ffn")}

    x_ts = []
    idxcs = []
    for c in range(NCORES):
        rows = np.r_[512 * c:512 * (c + 1), 512 * (15 - c):512 * (16 - c)]
        x_ts.append(np.ascontiguousarray(x[0, rows, :].T))
        idxc = np.zeros((128, 4), dtype=np.int32)
        idxc[:, 0] = 512 * c + 2 * pp
        idxc[:, 1] = 16 * pp + c
        idxc[:, 2] = 16 * pp + (15 - c)
        idxc[0:2, 3] = 4 * c
        idxcs.append(idxc)

    res1 = run_bass_kernel_spmd(
        _NC_CACHE["qkv"],
        [{"x_t": x_ts[c], "rw48": rw48, "w_qkv": w_qkv} for c in range(NCORES)],
        core_ids=list(range(NCORES)))
    qall = np.concatenate([np.asarray(res1.results[c]["qkv_own"]) for c in range(NCORES)], axis=0)

    res2 = run_bass_kernel_spmd(
        _NC_CACHE["attn"],
        [{"qkvh_all": qall, "idxs": idxcs[c]} for c in range(NCORES)],
        core_ids=list(range(NCORES)))
    oall = np.concatenate([np.asarray(res2.results[c]["o_own"]) for c in range(NCORES)], axis=0)

    res3 = run_bass_kernel_spmd(
        _NC_CACHE["ffn"],
        [{"x_t": x_ts[c], "oh_all": oall, "idxs": idxcs[c], "rw48": rw48,
          "w_out": w_out, "w_gate": w_gate, "w_up": w_up, "w_down": w_down,
          "sh_gate": sh_gate, "sh_up": sh_up, "sh_down": sh_down}
         for c in range(NCORES)],
        core_ids=list(range(NCORES)))

    out = np.empty((1, NSEQ, D), dtype=np.float32)
    for c in range(NCORES):
        y = np.asarray(res3.results[c]["y_t"], dtype=np.float32)
        out[0, 512 * c:512 * (c + 1), :] = y[:, 0:512].T
        out[0, 512 * (15 - c):512 * (16 - c), :] = y[:, 512:1024].T
    return out


if __name__ == "__main__":
    nc = build_program()
    print("build OK:", nc.n_instructions(), "instructions")
